# version 02 (final)
# baseline (speedup 1.0000x reference)
import math
import contextlib
import numpy as np
import ml_dtypes
import concourse.bass as bass
import concourse.mybir as mybir
from concourse.bass_utils import run_bass_kernel_spmd
from concourse.alu_op_type import AluOpType as OP

F32 = mybir.dt.float32
F16 = mybir.dt.float16
BF16 = mybir.dt.bfloat16
AF = mybir.ActivationFunctionType
AX = mybir.AxisListType.X
bfdt = ml_dtypes.bfloat16

D = 1024
T = 2048
DBG = False
BARRIER_V1 = True
BARRIER_DW = True
NSEQ = 2
NCORES = 8
DT = D // 128

ENGINES = ("tensor", "vector", "scalar", "gpsimd", "sync")

class Sem:
    __slots__ = ("h", "n", "name", "eng")
    def __init__(self, h, name, eng=None):
        self.h = h
        self.n = 0
        self.name = name
        self.eng = eng

class Guard:
    __slots__ = ("w", "r", "name")
    def __init__(self, name=""):
        self.w = None
        self.r = []
        self.name = name

class Prog:
    def __init__(self, nc, cx):
        self.nc = nc
        self.cx = cx
        self.steps = {e: [] for e in ENGINES}
        self._sems = {}
        self.es = {e: self.sem("pg_" + e, eng=e) for e in ENGINES}
        self.dsems = {}
        self.nsteps = 0

    def dsem(self, role):
        if role not in self.dsems:
            self.dsems[role] = self.sem("d_" + role)
        return self.dsems[role]

    def sem(self, name, eng=None):
        if name not in self._sems:
            h = self.cx.enter_context(self.nc.semaphore(name))
            self._sems[name] = Sem(h, name, eng)
        return self._sems[name]

    def sbuf(self, name, shape, dtype):
        return self.cx.enter_context(self.nc.sbuf_tensor(name, list(shape), dtype))

    def psum(self, name, shape, dtype=F32):
        return self.cx.enter_context(self.nc.psum_tensor(name, list(shape), dtype))

    def step(self, eng, build, reads=(), writes=(), extra_waits=(), dma=None):
        sem = self.dsem(dma) if dma else self.es[eng]
        inc_n = 16 if dma else 1
        waits = {}
        def add(tok):
            if tok is None:
                return
            s, v = tok
            if s.eng == eng:
                return
            if v > waits.get(s, -1):
                waits[s] = v
        for g in reads:
            add(g.w)
        for g in writes:
            add(g.w)
            for tok in g.r:
                add(tok)
        for tok in extra_waits:
            add(tok)
        sem.n += inc_n
        tok = (sem, sem.n)
        for g in reads:
            g.r.append(tok)
        for g in writes:
            g.w = tok
            g.r = []
        wl = [(s.h, v) for s, v in waits.items()]
        def fn(e, build=build, wl=wl, sem=sem, inc_n=inc_n):
            for h, v in wl:
                e.wait_ge(h, v)
            inst = build(e)
            assert inst is not None, "step build must return its last instruction"
            inst.then_inc(sem.h, inc_n)
        self.steps[eng].append(fn)
        self.nsteps += 1
        return tok

    def barrier(self):
        dtoks = [(s, s.n) for s in self.dsems.values()]
        if BARRIER_V1:
            toks = [(s, s.n) for s in self.es.values()] + dtoks
            for ename in ENGINES:
                self.step(ename, (lambda e: e.nop()),
                          extra_waits=[t for t in toks if t[0].eng != ename])
            return
        for ename in ENGINES:
            self.step(ename, (lambda e: e.nop()), extra_waits=(dtoks if BARRIER_DW else []))
        r1 = [(self.es[e], self.es[e].n) for e in ENGINES]
        for ename in ENGINES:
            self.step(ename, (lambda e: e.nop()),
                      extra_waits=[t for t in r1 if t[0].eng != ename])

    def finalize(self):
        blk = self.cx.enter_context(self.nc.Block())
        for ename in ENGINES:
            fns = self.steps[ename]
            def body(e, fns=fns):
                for fn in fns:
                    fn(e)
            getattr(blk, ename)(body)


def host_constants():
    c = {}
    c["ones_col"] = np.ones((128, 1), dtype=np.float32)
    c["ones_row"] = np.ones((1, 128), dtype=np.float32)
    c["ident"] = np.eye(128, dtype=np.float32)
    DIM_HEAD, WINDOW = 64, 256
    half = np.arange(0, DIM_HEAD, 2, dtype=np.float64)
    inv_freq = 1.0 / 10000.0 ** (half / DIM_HEAD)
    tt = np.arange(T, dtype=np.float64)
    freqs = np.concatenate([tt[:, None] * inv_freq[None, :]] * 2, -1)
    base_scale = (half + 0.4 * DIM_HEAD) / (1.4 * DIM_HEAD)
    power = (tt - T // 2) / (WINDOW // 2)
    scale = np.concatenate([base_scale[None, :] ** power[:, None]] * 2, -1)
    cos, sin = np.cos(freqs), np.sin(freqs)
    for name, tab in [("cs", cos * scale), ("ss", sin * scale),
                      ("ci", cos / scale), ("si", sin / scale)]:
        tb = tab.T.astype(np.float32)
        c["rot_" + name] = np.concatenate([tb, tb], axis=0)
    P64 = np.zeros((64, 64), dtype=np.float32)
    for e in range(32):
        P64[e + 32, e] = -1.0
        P64[e, e + 32] = 1.0
    Pd = np.zeros((128, 128), dtype=np.float32)
    Pd[:64, :64] = P64
    Pd[64:, 64:] = P64
    c["prot"] = Pd
    Es = np.zeros((DT, 128, 16), dtype=np.float32)
    Fs = np.zeros((DT, 16, 128), dtype=np.float32)
    for tdt in range(DT):
        Es[tdt, 0:64, 2 * tdt] = 1.0
        Es[tdt, 64:128, 2 * tdt + 1] = 1.0
        Fs[tdt, 2 * tdt, 0:64] = 1.0
        Fs[tdt, 2 * tdt + 1, 64:128] = 1.0
    c["esel"] = Es
    c["fsel"] = Fs
    jl = np.arange(128)[:, None]
    il = np.arange(256)[None, :]
    keep = np.zeros((4, 128, 256), dtype=bool)
    keep[0] = (jl >= il)
    keep[1] = (jl >= il - 128)
    keep[2] = (jl <= il)
    keep[3] = (jl <= il - 128)
    mneg = np.where(keep, 0.0, -1e30).astype(np.float32)
    c["maskneg"] = np.concatenate([mneg[i] for i in range(4)], axis=1)
    c["fsel2"] = np.zeros((2, 128), dtype=np.float32)
    c["fsel2"][0, 0:64] = 1.0
    c["fsel2"][1, 64:128] = 1.0
    c["esel2"] = np.zeros((128, 2), dtype=np.float32)
    c["esel2"][0:64, 0] = 1.0
    c["esel2"][64:128, 1] = 1.0
    c["halfones"] = np.zeros((2, 1, 128), dtype=np.float32)
    c["halfones"][0, 0, 0:64] = 1.0
    c["halfones"][1, 0, 64:128] = 1.0
    ec2 = np.zeros((2, 2, 128), dtype=np.float32)
    ec2[0, 0, :] = 1.0
    ec2[1, 1, :] = 1.0
    c["ec2"] = np.concatenate([ec2[0], ec2[1]], axis=1)
    EC = np.zeros((4, 4, 128), dtype=np.float32)
    for cc in range(4):
        EC[cc, cc, :] = 1.0
    c["ec4"] = np.concatenate([EC[i] for i in range(4)], axis=1)
    EH16 = np.zeros((16, 16 * 128), dtype=np.float32)
    for hh in range(16):
        EH16[hh, hh * 128:(hh + 1) * 128] = 1.0
    c["eh16cat"] = EH16
    EH = np.zeros((16, 16, 128), dtype=np.float32)
    for h in range(16):
        EH[h, h, :] = 1.0
    c["eh16"] = EH
    NUM_BUCKETS, MAX_DISTANCE = 320, 800
    m = np.arange(4095)
    rel = 2047 - m
    nb = NUM_BUCKETS // 2
    buckets = (rel > 0).astype(np.int64) * nb
    rel_abs = np.abs(rel)
    max_exact = nb // 2
    is_small = rel_abs < max_exact
    large = np.log(np.maximum(rel_abs, 1).astype(np.float64) / max_exact) / math.log(MAX_DISTANCE / max_exact) * (nb - max_exact)
    large = (max_exact + large).astype(np.int64)
    large = np.minimum(large, nb - 1)
    bucket_of_m = buckets + np.where(is_small, rel_abs, large)
    OH = np.zeros((320, 4095), dtype=np.float32)
    OH[bucket_of_m, m] = 1.0
    c["bucket_oh"] = OH
    c["J"] = np.eye(128, dtype=np.float32)[::-1].copy()
    GS = np.zeros((8, 2), dtype=np.float32)
    GS[0:4, 0] = 1.0
    GS[4:8, 1] = 1.0
    c["gsel"] = GS
    return c


A2_COLS = 30464
A4_COLS = 4352

class Resident:
    def __init__(self, P, dram):
        self.P = P
        self.XT = P.sbuf("XT", [128, NSEQ * DT * T], F32)
        self.gXT = Guard("XT")
        self.A2 = P.sbuf("A2", [128, A2_COLS], BF16)
        self.A4 = P.sbuf("A4", [128, A4_COLS], F32)
        self._a2off = 0
        self._a4off = 0
        self.PB = [P.psum(f"pb{i}", [128, 512]) for i in range(8)]
        self.gPB = [Guard(f"pb{i}") for i in range(8)]
        self.ones_col = P.sbuf("cc_ones_col", [128, 1], BF16)
        self.ones_row = P.sbuf("cc_ones_row", [1, 128], F32)
        self.ones_row_bf = P.sbuf("cc_ones_row_bf", [1, 128], BF16)
        self.ident = P.sbuf("cc_ident", [128, 128], BF16)
        self.gconst = Guard("const")
        for dst, src in [(self.ones_col, dram["c_ones_col"]), (self.ones_row_bf, dram["c_ones_row"]),
                         (self.ident, dram["c_ident"])]:
            P.step("gpsimd", (lambda e, dst=dst, src=src: e.dma_start(out=dst[:], in_=src)),
                   writes=[self.gconst], dma="cc")
        P.step("sync", lambda e: e.dma_start(out=self.ones_row[:], in_=dram["c_ones_row"]),
               writes=[self.gconst], dma="cch")
        self.fsel2 = P.sbuf("cc_fsel2", [2, 128], BF16)
        self.esel2 = P.sbuf("cc_esel2", [128, 2], BF16)
        self.ec2 = P.sbuf("cc_ec2", [2, 2 * 128], BF16)
        self.halfones0 = P.sbuf("cc_halfones0", [1, 128], F32)
        self.halfones1 = P.sbuf("cc_halfones1", [1, 128], F32)
        self.protsb = P.sbuf("cc_prot", [128, 128], BF16)
        P.step("gpsimd", lambda e: e.dma_start(out=self.fsel2[:], in_=dram["c_fsel2"]),
               writes=[self.gconst], dma="cc")
        P.step("gpsimd", lambda e: e.dma_start(out=self.esel2[:], in_=dram["c_esel2"]),
               writes=[self.gconst], dma="cc")
        P.step("gpsimd", lambda e: e.dma_start(out=self.ec2[:], in_=dram["c_ec2"]),
               writes=[self.gconst], dma="cc")
        P.step("sync", lambda e: e.dma_start(out=self.halfones0[:], in_=dram["c_halfones"][0]),
               writes=[self.gconst], dma="cch")
        P.step("sync", lambda e: e.dma_start(out=self.halfones1[:], in_=dram["c_halfones"][1]),
               writes=[self.gconst], dma="cch")
        P.step("gpsimd", lambda e: e.dma_start(out=self.protsb[:], in_=dram["c_prot"]),
               writes=[self.gconst], dma="cc")

        self.eps = P.sbuf("cc_eps", [1, 1], F32)
        self.mhalf = P.sbuf("cc_mhalf", [1, 1], F32)
        self.cm1 = P.sbuf("cc_cm1", [1, 1], F32)
        self.mhalf2 = P.sbuf("cc_mhalf2", [2, 1], F32)
        self.c8 = P.sbuf("cc_c8", [128, 1], F32)
        self.gc = Guard("cc_small")
        P.step("vector", lambda e: e.memset(self.eps[:], 1e-5), writes=[self.gc])
        P.step("vector", lambda e: e.memset(self.mhalf[:], -0.5), writes=[self.gc])
        P.step("vector", lambda e: e.memset(self.cm1[:], -1.0), writes=[self.gc])
        P.step("vector", lambda e: e.memset(self.mhalf2[:], -0.5), writes=[self.gc])
        P.step("vector", lambda e: e.memset(self.c8[:], 8.0), writes=[self.gc])
        self.c0125 = P.sbuf("cc_c0125", [128, 1], F32)
        P.step("vector", lambda e: e.memset(self.c0125[:], 0.125), writes=[self.gc])

    def new_section(self, P):
        P.barrier()
        self.carve_reset()

    def carve_reset(self):
        self._a2off = 0
        self._a4off = 0

    def carve2(self, ncols, dtype=BF16):
        off = self._a2off
        self._a2off += (ncols + 15) & ~15
        assert self._a2off <= A2_COLS, f"A2 arena overflow: {self._a2off}"
        ap = self.A2[:, off:off + ncols]
        if dtype is not BF16:
            ap = ap.bitcast(dtype)
        return ap

    def carve4(self, ncols):
        off = self._a4off
        self._a4off += (ncols + 15) & ~15
        assert self._a4off <= A4_COLS, f"A4 arena overflow: {self._a4off}"
        return self.A4[:, off:off + ncols]

    def xT(self, s, dtile):
        off = (s * DT + dtile) * T
        return self.XT[:, off:off + T]

    def load_x(self, P, xdram):
        for s in range(NSEQ):
            src = bass.AP(xdram.tensor, s * D * T, [[T, 128], [128 * T, DT], [1, T]])
            P.step("sync", (lambda e, s=s, src=src: e.dma_start(
                out=self.XT[:, s * DT * T:(s + 1) * DT * T], in_=src)),
                writes=[self.gXT], dma="xin")

    def store_x(self, P, odram):
        for s in range(NSEQ):
            dst = bass.AP(odram.tensor, s * D * T, [[T, 128], [128 * T, DT], [1, T]])
            P.step("sync", (lambda e, s=s, dst=dst: e.dma_start(
                out=dst, in_=self.XT[:, s * DT * T:(s + 1) * DT * T])),
                reads=[self.gXT], dma="xout")
        qtok = (P.dsem("xout"), P.dsem("xout").n)
        P.step("sync", (lambda e: e.nop()), extra_waits=[qtok])


LCH = 512

class LNArea:
    def __init__(self, P, R, a2alias=None):
        if a2alias is None:
            self.zbf = [R.carve2(LCH) for _ in range(2)]
            self.zsq = [R.carve2(LCH) for _ in range(2)]
            self.gzbf = [Guard() for _ in range(2)]
            self.gzsq = [Guard() for _ in range(2)]
        else:
            self.zbf = [a2alias[0][0], a2alias[1][0]]
            self.zsq = [a2alias[2][0], a2alias[3][0]]
            self.gzbf = [a2alias[0][1], a2alias[1][1]]
            self.gzsq = [a2alias[2][1], a2alias[3][1]]
        self.rowbuf = R.carve4(4 * LCH)
        self.grows = Guard("ln_rows")
        self.rstdB = R.carve4(LCH)
        self.msB = R.carve4(LCH)
        self.gB = Guard("ln_B")
        self.eps = R.eps
        self.mhalf = R.mhalf
        self.gc = R.gc

    def row(self, i):
        return self.rowbuf[0:1, i * LCH:(i + 1) * LCH]

def emit_ln_chunk(P, R, ln, src_aps, gsrc, out_aps, gout, pst_banks=(7, 4), bc_banks=(5, 6),
                  gamma=None, beta=None, ggb=None):
    pst0, pst1 = R.PB[pst_banks[0]], R.PB[pst_banks[1]]
    gpst0, gpst1 = R.gPB[pst_banks[0]], R.gPB[pst_banks[1]]
    pb0, pb1 = R.PB[bc_banks[0]], R.PB[bc_banks[1]]
    gpb0, gpb1 = R.gPB[bc_banks[0]], R.gPB[bc_banks[1]]
    for d in range(DT):
        b = d % 2
        P.step("vector", (lambda e, d=d, b=b: e.tensor_copy(ln.zbf[b][:], src_aps[d])),
               reads=[gsrc], writes=[ln.gzbf[b]])
        P.step("vector", (lambda e, b=b: e.tensor_tensor(ln.zsq[b][:], ln.zbf[b][:], ln.zbf[b][:], OP.mult)),
               reads=[ln.gzbf[b]], writes=[ln.gzsq[b]])
        def statmm(e, d=d, b=b):
            e.matmul(pst0[0:1, :], R.ones_col[:], ln.zbf[b][:], start=(d == 0), stop=(d == DT - 1))
            return e.matmul(pst1[0:1, :], R.ones_col[:], ln.zsq[b][:], start=(d == 0), stop=(d == DT - 1))
        P.step("tensor", statmm, reads=[ln.gzbf[b], ln.gzsq[b], R.gconst], writes=[gpst0, gpst1])
    P.step("vector", (lambda e: e.tensor_single_scalar(ln.row(0), pst0[0:1, :], 1.0 / D, OP.mult)),
           reads=[gpst0], writes=[ln.grows])
    P.step("vector", (lambda e: e.tensor_single_scalar(ln.row(1), pst1[0:1, :], 1.0 / D, OP.mult)),
           reads=[gpst1], writes=[ln.grows])
    P.step("vector", (lambda e: e.tensor_tensor(ln.row(3), ln.row(0), ln.row(0), OP.mult)),
           reads=[ln.grows], writes=[ln.grows])
    P.step("vector", (lambda e: e.tensor_tensor(ln.row(1), ln.row(1), ln.row(3), OP.subtract)),
           reads=[ln.grows], writes=[ln.grows])
    P.step("scalar", (lambda e: e.activation(ln.row(2), ln.row(1), AF.Ln, bias=ln.eps[0:1, :])),
           reads=[ln.grows, ln.gc], writes=[ln.grows])
    P.step("scalar", (lambda e: e.activation(ln.row(2), ln.row(2), AF.Exp, scale=ln.mhalf[0:1, :])),
           reads=[ln.gc], writes=[ln.grows])
    P.step("vector", (lambda e: e.tensor_tensor(ln.row(3), ln.row(0), ln.row(2), OP.mult)),
           reads=[ln.grows], writes=[ln.grows])
    def bc0(e):
        return e.matmul(pb0[:], R.ones_row[0:1, :], ln.row(2), start=True, stop=True)
    def bc1(e):
        return e.matmul(pb1[:], R.ones_row[0:1, :], ln.row(3), start=True, stop=True)
    P.step("tensor", bc0, reads=[ln.grows, R.gconst], writes=[gpb0])
    P.step("tensor", bc1, reads=[ln.grows, R.gconst], writes=[gpb1])
    P.step("vector", (lambda e: e.tensor_copy(ln.rstdB[:], pb0[:])), reads=[gpb0], writes=[ln.gB])
    P.step("vector", (lambda e: e.tensor_copy(ln.msB[:], pb1[:])), reads=[gpb1], writes=[ln.gB])
    for d in range(DT):
        P.step("vector", (lambda e, d=d: e.tensor_tensor(out_aps[d], src_aps[d], ln.rstdB[:], OP.mult)),
               reads=[gsrc, ln.gB], writes=[gout])
        P.step("vector", (lambda e, d=d: e.tensor_tensor(out_aps[d], out_aps[d], ln.msB[:], OP.subtract)),
               reads=[ln.gB], writes=[gout])
        if gamma is not None:
            P.step("vector", (lambda e, d=d: e.tensor_scalar(out_aps[d], out_aps[d], gamma[d], beta[d],
                                                             OP.mult, OP.add)),
                   reads=[ggb], writes=[gout])


def emit_phase_a(P, R, dram):
    CH = 512
    NCH = T // CH
    w_in, cwT, cbT, w_out = dram["a_w_in"], dram["a_cwT"], dram["a_cbT"], dram["a_w_out"]
    R.carve_reset()
    ln = LNArea(P, R)
    xbf = R.carve2(DT * CH)
    wib = [R.carve2(8 * 512) for _ in range(1)]
    wob = [R.carve2(16 * 128) for _ in range(1)]
    U = R.carve2(16 * (CH + 3))
    Y = R.carve2(16 * CH)
    sg = [R.carve2(CH) for _ in range(2)]
    acc = [R.carve4(CH) for _ in range(2)]
    cw = R.carve4(16 * 4)
    cb = R.carve4(16)
    gxbf, gU, gY, gcw = Guard(), Guard(), Guard(), Guard()
    gwib = [Guard() for _ in range(1)]
    gwob = [Guard() for _ in range(1)]
    gsg = [Guard() for _ in range(2)]
    gacc = [Guard() for _ in range(2)]

    P.step("sync", lambda e: e.dma_start(out=cw, in_=cwT), writes=[gcw], dma="a_cw")
    P.step("sync", lambda e: e.dma_start(out=cb, in_=cbT), writes=[gcw], dma="a_cw")

    for s in range(NSEQ):
        for cidx in range(NCH):
            tok = slice(cidx * CH, (cidx + 1) * CH)
            for d in range(DT):
                P.step("vector", (lambda e, d=d, tok=tok, s=s: e.tensor_copy(
                    xbf[:, d * CH:(d + 1) * CH], R.xT(s, d)[:, tok])),
                    reads=[R.gXT], writes=[gxbf])
            if cidx == 0:
                def padz(e):
                    last = None
                    for ct in range(16):
                        last = e.memset(U[:, ct * (CH + 3): ct * (CH + 3) + 3], 0.0)
                    return last
                P.step("vector", padz, reads=[gU], writes=[gU])
            else:
                def padc(e):
                    last = None
                    for ct in range(16):
                        last = e.tensor_copy(U[:, ct * (CH + 3): ct * (CH + 3) + 3],
                                             U[:, ct * (CH + 3) + CH: ct * (CH + 3) + CH + 3])
                    return last
                P.step("vector", padc, reads=[gU], writes=[gU])
            for half in range(2):
                for fc in range(4):
                    fcg = half * 4 + fc
                    buf, gb = wib[0], gwib[0]
                    src = bass.AP(w_in.tensor, fcg * 512, [[4096, 128], [128 * 4096, 8], [1, 512]])
                    P.step("gpsimd", (lambda e, buf=buf, src=src: e.dma_start(out=buf, in_=src)),
                           writes=[gb], dma="a_wib")
                    for fb in range(4):
                        bk = (fcg * 4 + fb) % 2
                        ct = ((fcg * 512 + fb * 128) // 128) % 16
                        def mmblk(e, buf=buf, fb=fb, bk=bk):
                            last = None
                            for d in range(DT):
                                last = e.matmul(R.PB[bk][:], buf[:, d * 512 + fb * 128: d * 512 + fb * 128 + 128],
                                                xbf[:, d * CH:(d + 1) * CH],
                                                start=(d == 0), stop=(d == DT - 1))
                            return last
                        P.step("tensor", mmblk, reads=[gb, gxbf], writes=[R.gPB[bk]])
                        if half == 0:
                            P.step("vector", (lambda e, bk=bk, ct=ct: e.tensor_copy(
                                U[:, ct * (CH + 3) + 3: ct * (CH + 3) + 3 + CH], R.PB[bk][:])),
                                reads=[R.gPB[bk]], writes=[gU])
                        else:
                            sgb = fb % 2
                            P.step("scalar", (lambda e, bk=bk, sgb=sgb: e.activation(
                                sg[sgb], R.PB[bk][:], AF.Sigmoid)),
                                reads=[R.gPB[bk]], writes=[gsg[sgb]])
                            P.step("vector", (lambda e, ct=ct, sgb=sgb: e.tensor_tensor(
                                U[:, ct * (CH + 3) + 3: ct * (CH + 3) + 3 + CH],
                                U[:, ct * (CH + 3) + 3: ct * (CH + 3) + 3 + CH],
                                sg[sgb], OP.mult)),
                                reads=[gsg[sgb]], writes=[gU])
            for ct in range(16):
                ab = ct % 2
                ub = ct * (CH + 3)
                def conv(e, ub=ub, ct=ct, ab=ab):
                    e.tensor_scalar(acc[ab], U[:, ub: ub + CH], cw[:, ct * 4: ct * 4 + 1], None, OP.mult)
                    last = None
                    for k in (1, 2, 3):
                        last = e.scalar_tensor_tensor(acc[ab], U[:, ub + k: ub + k + CH],
                                                      cw[:, ct * 4 + k: ct * 4 + k + 1],
                                                      acc[ab], OP.mult, OP.add)
                    return last
                P.step("vector", conv, reads=[gU, gcw], writes=[gacc[ab]])
                P.step("scalar", (lambda e, ct=ct, ab=ab: e.activation(
                    Y[:, ct * CH:(ct + 1) * CH], acc[ab], AF.Silu, bias=cb[:, ct:ct + 1])),
                    reads=[gacc[ab], gcw], writes=[gY])
            for dtile in range(8):
                buf, gb = wob[0], gwob[0]
                src = bass.AP(w_out.tensor, dtile * 128, [[1024, 128], [128 * 1024, 16], [1, 128]])
                P.step("gpsimd", (lambda e, buf=buf, src=src: e.dma_start(out=buf, in_=src)),
                       writes=[gb], dma="a_wob")
                bk = 2 + (dtile % 2)
                def mmout(e, buf=buf, bk=bk):
                    last = None
                    for ct in range(16):
                        last = e.matmul(R.PB[bk][:], buf[:, ct * 128: ct * 128 + 128],
                                        Y[:, ct * CH:(ct + 1) * CH],
                                        start=(ct == 0), stop=(ct == 15))
                    return last
                P.step("tensor", mmout, reads=[gb, gY], writes=[R.gPB[bk]])
                P.step("vector", (lambda e, bk=bk, dtile=dtile, s=s, tok=tok: e.tensor_tensor(
                    R.xT(s, dtile)[:, tok], R.PB[bk][:], R.xT(s, dtile)[:, tok], OP.add)),
                    reads=[R.gPB[bk]], writes=[R.gXT])
            src_aps = [R.xT(s, d)[:, tok] for d in range(DT)]
            emit_ln_chunk(P, R, ln, src_aps, R.gXT, src_aps, R.gXT)



def _shift_plan(R, s, d, c0, CH, bcol=None, gbc=None):
    if d < 4:
        return ("full", R.xT(s, d)[:, c0:c0 + CH], None)
    if c0 == 0:
        return ("z0", R.xT(s, d)[:, 0:CH - 1], None)
    ci = (c0 // 512) - 1
    return ("b0", R.xT(s, d)[:, c0:c0 + CH - 1], (bcol[:, (d - 4) * 3 + ci:(d - 4) * 3 + ci + 1], gbc))

def _save_bcols(P, R, s, bcol, gbc):
    def sv(e):
        last = None
        for d in range(4, 8):
            last = e.tensor_copy(bcol[:, (d - 4) * 3:(d - 4) * 3 + 3],
                                 bass.AP(R.XT, R.xT(s, d).offset + 511,
                                         [[int(R.XT.shape[1]), 128], [512, 3]]))
        return last
    P.step("vector", sv, reads=[R.gXT], writes=[gbc])

def _copy_shifted(P, R, dst_ap, plan, gdst, extra_reads=()):
    kind, ap, bc = plan
    if kind == "full":
        P.step("vector", (lambda e, dst_ap=dst_ap, ap=ap: e.tensor_copy(dst_ap, ap)),
               reads=[R.gXT] + list(extra_reads), writes=[gdst])
    elif kind == "z0":
        def cp(e, dst_ap=dst_ap, ap=ap):
            e.memset(dst_ap[:, 0:1], 0.0)
            return e.tensor_copy(dst_ap[:, 1:], ap)
        P.step("vector", cp, reads=[R.gXT] + list(extra_reads), writes=[gdst])
    else:
        bca, gbc = bc
        def cp(e, dst_ap=dst_ap, ap=ap, bca=bca):
            e.tensor_copy(dst_ap[:, 0:1], bca)
            return e.tensor_copy(dst_ap[:, 1:], ap)
        P.step("vector", cp, reads=[R.gXT, gbc] + list(extra_reads), writes=[gdst])

WIN = 256

def emit_b_attn(P, R, dram, s, l):
    R.new_section(P)
    wqkv = dram["b_wqkv"]
    wo = dram["b_wo"]
    rotA = R.carve2(512)
    rotB = R.carve2(512)
    wq = R.carve2(8 * 512)
    xsbf = R.carve2(8 * 512)
    qav = R.carve2(8 * 512)
    kring = R.carve2(8 * 3 * WIN)
    vext = R.carve2(6 * 1040)
    Eb = R.carve2(4 * WIN)
    runmax = R.carve2(2 * WIN)
    prepscr = R.carve2(1024)
    bufA = prepscr[:, 0:512]
    bufB = prepscr[:, 512:1024]
    t1 = R.carve2(512)
    negc2b = t1[:, 0:16]
    nmTb = t1[:, 16:144]
    negc2 = R.carve2(16)
    nmT = R.carve2(128)
    masksb = R.carve2(4 * WIN)
    woB = xsbf[:, 0:1024]
    fac = R.carve4(512)
    facbf_t = R.carve2(512)
    reciprow = R.carve4(512)
    lnrow = R.carve4(256)
    lnrowb = R.carve4(256)
    qscol = R.carve4(16)
    bcol = R.carve4(16)
    grot, gwq, gxsbf, gkr, gvx, gE = Guard(), Guard(), Guard(), Guard(), Guard(), Guard()
    grm, gbufA, gbufB, gt1, gnegc, gnmT, gmask = Guard(), Guard(), Guard(), Guard(), Guard(), Guard(), Guard()
    gfac, grecip, gqs = Guard(), Guard(), Guard()
    grmB, gE2 = Guard(), Guard()
    gQ = [[[Guard(), Guard()] for _ in range(2)] for _ in range(DT)]

    P.step("gpsimd", (lambda e: e.dma_start(out=masksb[:], in_=dram["c_maskneg"])),
           writes=[gmask], dma="b_mask")
    P.step("gpsimd", (lambda e: e.dma_start(out=qscol[:, 0:1],
                                            in_=dram["b_qscol"][l:l + 1, :].rearrange("a b -> b a"))),
           writes=[gqs], dma="b_qs")
    P.step("gpsimd", (lambda e: e.dma_start(out=qscol[:, 1:2],
                                            in_=dram["b_kscol"][l:l + 1, :].rearrange("a b -> b a"))),
           writes=[gqs], dma="b_qs")

    def load_tables(qk, c0):
        names = ("c_rot_cs", "c_rot_ss") if qk == 0 else ("c_rot_ci", "c_rot_si")
        P.step("gpsimd", (lambda e, n=names[0], c0=c0: e.dma_start(out=rotA[:], in_=dram[n][:, c0:c0 + 512])),
               writes=[grot], dma="b_rotA")
        P.step("gpsimd", (lambda e, n=names[1], c0=c0: e.dma_start(out=rotB[:], in_=dram[n][:, c0:c0 + 512])),
               writes=[grot], dma="b_rotB")

    gbc = Guard()
    _save_bcols(P, R, s, bcol, gbc)

    for icg in range(4):
        c0 = icg * 512
        w0, w1 = 2 * icg, 2 * icg + 1
        for d in range(DT):
            _copy_shifted(P, R, xsbf[:, d * 512:(d + 1) * 512],
                          _shift_plan(R, s, d, c0, 512, bcol, gbc), gxsbf)
        for qk in range(2):
            load_tables(qk, c0)
            scol = qscol[:, qk:qk + 1]
            for fb in range(DT):
                fglob = qk * 1024 + fb * 128
                if fb % 4 == 0:
                    fcg = fglob // 512
                    src = bass.AP(wqkv.tensor, l * 1024 * 3072 + fcg * 512,
                                  [[3072, 128], [128 * 3072, 8], [1, 512]])
                    P.step("gpsimd", (lambda e, src=src: e.dma_start(out=wq[:], in_=src)),
                           writes=[gwq], dma="b_wq")
                bk = 4 + fb % 2
                wcol = (fglob % 512)
                def qmm(e, wcol=wcol, bk=bk):
                    last = None
                    for d in range(DT):
                        last = e.matmul(R.PB[bk][:], wq[:, d * 512 + wcol: d * 512 + wcol + 128],
                                        xsbf[:, d * 512:(d + 1) * 512],
                                        start=(d == 0), stop=(d == DT - 1))
                    return last
                P.step("tensor", qmm, reads=[gwq, gxsbf], writes=[R.gPB[bk]])
                P.step("scalar", (lambda e, bk=bk: e.copy(bufB[:], R.PB[bk][:])),
                       reads=[R.gPB[bk]], writes=[gbufB])
                P.step("scalar", (lambda e, bk=bk: e.activation(bufA[:], R.PB[bk][:], AF.Square)),
                       reads=[R.gPB[bk]], writes=[gbufA])
                P.step("tensor", (lambda e: e.matmul(R.PB[6][0:2, :], R.esel2[:], bufA[:],
                                                     start=True, stop=True)),
                       reads=[gbufA, R.gconst], writes=[R.gPB[6]])
                P.step("vector", (lambda e: e.tensor_single_scalar(fac[0:2, :], R.PB[6][0:2, :], 1e-24, OP.max)),
                       reads=[R.gPB[6]], writes=[gfac])
                P.step("scalar", (lambda e: e.activation(fac[0:2, :], fac[0:2, :], AF.Ln)),
                       writes=[gfac])
                P.step("scalar", (lambda e: e.activation(fac[0:2, :], fac[0:2, :], AF.Exp, scale=R.mhalf2[:])),
                       reads=[R.gc], writes=[gfac])
                P.step("vector", (lambda e: e.tensor_copy(facbf_t[0:2, :], fac[0:2, :])),
                       reads=[gfac], writes=[gfac])
                facbf = facbf_t[0:2, :]
                P.step("tensor", (lambda e, facbf=facbf: e.matmul(R.PB[7][:], R.fsel2[:], facbf,
                                                                  start=True, stop=True)),
                       reads=[gfac, R.gconst], writes=[R.gPB[7]])
                P.step("vector", (lambda e, scol=scol: e.scalar_tensor_tensor(
                    bufA[:], R.PB[7][:], scol, bufB[:], OP.mult, OP.mult)),
                    reads=[R.gPB[7], gqs, gbufB], writes=[gbufA])
                P.step("tensor", (lambda e, bk=bk: e.matmul(R.PB[bk][:], R.protsb[:], bufA[:],
                                                            start=True, stop=True)),
                       reads=[gbufA, R.gconst], writes=[R.gPB[bk]])
                P.step("vector", (lambda e: e.tensor_tensor(t1[:], bufA[:], rotA[:], OP.mult)),
                       reads=[gbufA, grot], writes=[gt1])
                P.step("vector", (lambda e, bk=bk: e.tensor_tensor(bufB[:], R.PB[bk][:], rotB[:], OP.mult)),
                       reads=[R.gPB[bk], grot], writes=[gbufB])
                if qk == 0:
                    P.step("vector", (lambda e, fb=fb: e.tensor_tensor(
                        qav[:, fb * 512:(fb + 1) * 512], t1[:], bufB[:], OP.add)),
                        reads=[gt1, gbufB], writes=gQ[fb][0] + gQ[fb][1])
                else:
                    s0 = w0 % 3
                    s1 = w1 % 3
                    P.step("vector", (lambda e, fb=fb, s0=s0: e.tensor_tensor(
                        kring[:, fb * 768 + s0 * WIN: fb * 768 + s0 * WIN + WIN],
                        t1[:, 0:256], bufB[:, 0:256], OP.add)),
                        reads=[gt1, gbufB], writes=[gkr])
                    P.step("vector", (lambda e, fb=fb, s1=s1: e.tensor_tensor(
                        kring[:, fb * 768 + s1 * WIN: fb * 768 + s1 * WIN + WIN],
                        t1[:, 256:512], bufB[:, 256:512], OP.add)),
                        reads=[gt1, gbufB], writes=[gkr])
        for fc in range(2):
            src = bass.AP(wqkv.tensor, l * 1024 * 3072 + 2048 + fc * 512,
                          [[3072, 128], [128 * 3072, 8], [1, 512]])
            P.step("gpsimd", (lambda e, src=src: e.dma_start(out=wq[:], in_=src)),
                   writes=[gwq], dma="b_wq")
            for tb in range(4):
                bk = 4 + tb % 2
                def vmm(e, tb=tb, bk=bk):
                    last = None
                    for d in range(DT):
                        last = e.matmul(R.PB[bk][:], xsbf[:, d * 512 + tb * 128: d * 512 + tb * 128 + 128],
                                        wq[:, d * 512:(d + 1) * 512],
                                        start=(d == 0), stop=(d == DT - 1))
                    return last
                P.step("tensor", vmm, reads=[gwq, gxsbf], writes=[R.gPB[bk]])
                w = (c0 + tb * 128) // WIN
                vti = (w % 3) * 2 + (tb % 2)
                base = vti * 1040 + (8 * fc) * 65
                dst = vext[:, base: base + 8 * 65].rearrange("p (h e) -> p h e", e=65)[:, :, 0:64]
                P.step("vector", (lambda e, bk=bk, dst=dst: e.tensor_copy(dst, R.PB[bk][:].rearrange(
                    "p (h e) -> p h e", e=64))),
                    reads=[R.gPB[bk]], writes=[gvx])
                if fc == 1:
                    ones_dst = vext[:, vti * 1040: vti * 1040 + 1040].rearrange(
                        "p (h e) -> p h e", e=65)[:, :, 64:65]
                    P.step("vector", (lambda e, ones_dst=ones_dst: e.memset(ones_dst, 1.0)),
                           writes=[gvx])
        if DBG and s == 0 and l == 0 and icg == 0:
            gdbg = Guard()
            P.step("gpsimd", (lambda e: e.dma_start(out=dram["dbg"][:, 0:4096], in_=xsbf[:])),
                   reads=[gxsbf], writes=[gdbg], dma="dbg1")
            P.step("gpsimd", (lambda e: e.dma_start(out=dram["dbg"][:, 4096:8192], in_=qav[:])),
                   reads=[gQ[0][0]], writes=[gdbg], dma="dbg1")
            P.step("gpsimd", (lambda e: e.dma_start(out=dram["dbg"][:, 8192:14336], in_=kring[:])),
                   reads=[gkr], writes=[gdbg], dma="dbg1")
            P.step("gpsimd", (lambda e: e.dma_start(out=dram["dbg"][:, 14336:20576], in_=vext[:])),
                   reads=[gvx], writes=[gdbg], dma="dbg1")
        for h in range(16):
            dt_, hf = h // 2, h % 2
            prow = hf * 64
            for wi, w in enumerate((w0, w1)):
                par = wi
                sbk = (0, 1) if par == 0 else (2, 3)
                mb = 4 if par == 0 else 6
                ab = 5 if par == 0 else 7
                Eap = Eb if par == 0 else prepscr
                gEi = [gE] if par == 0 else [gE2, gbufA, gbufB]
                ngc = negc2 if par == 0 else negc2b
                nmA = nmT if par == 0 else nmTb
                lnr = lnrow if par == 0 else lnrowb
                ggn = gnegc if par == 0 else gt1
                ggm = gnmT if par == 0 else gt1
                grmP = grm if par == 0 else grmB
                rho0 = 2 if w == 0 else 0
                def sims(e, w=w, rho0=rho0, dt_=dt_, prow=prow, wi=wi, sbk=sbk):
                    last = None
                    for rho in range(rho0, 4):
                        jw = w - 1 + rho // 2
                        slot = jw % 3
                        kap = kring[prow:prow + 64,
                                    dt_ * 768 + slot * WIN + (rho % 2) * 128:
                                    dt_ * 768 + slot * WIN + (rho % 2) * 128 + 128]
                        qap = qav[prow:prow + 64, dt_ * 512 + wi * 256: dt_ * 512 + wi * 256 + 256]
                        last = e.matmul(R.PB[sbk[rho // 2]][:, (rho % 2) * 256:(rho % 2) * 256 + 256],
                                        kap, qap, start=(rho % 2 == 0), stop=False)
                    return last
                wr = [R.gPB[b] for b in (sbk if rho0 == 0 else (sbk[1],))]
                P.step("tensor", sims, reads=[gkr, gQ[dt_][hf][wi]], writes=wr)
                def maskadd(e, rho0=rho0, sbk=sbk):
                    last = None
                    for bi in range(rho0 // 2, 2):
                        sl = R.PB[sbk[bi]][:, 0:512]
                        last = e.tensor_tensor(sl, sl, masksb[:, bi * 512:(bi + 1) * 512], OP.add)
                    return last
                P.step("vector", maskadd, reads=[gmask], writes=wr)
                rm = runmax[:, par * WIN:(par + 1) * WIN]
                def mx(e, rho0=rho0, rm=rm, sbk=sbk):
                    first = True
                    last = None
                    for rho in range(rho0, 4):
                        sl = R.PB[sbk[rho // 2]][:, (rho % 2) * 256:(rho % 2) * 256 + 256]
                        if first:
                            last = e.tensor_copy(rm, sl)
                            first = False
                        else:
                            last = e.tensor_tensor(rm, sl, rm, OP.max)
                    return last
                P.step("vector", mx, reads=wr, writes=[grmP])
                def tp(e, rm=rm, mb=mb):
                    e.matmul(R.PB[mb][:, 0:128], rm[:, 0:128], R.ident[:], start=True, stop=True)
                    return e.matmul(R.PB[mb][:, 128:256], rm[:, 128:256], R.ident[:], start=True, stop=True)
                P.step("tensor", tp, reads=[grmP, R.gconst], writes=[R.gPB[mb]])
                def rd(e, mb=mb, ngc=ngc):
                    e.tensor_reduce(ngc[:, 0:1], R.PB[mb][:, 0:128], axis=AX, op=OP.max, negate=True)
                    return e.tensor_reduce(ngc[:, 1:2], R.PB[mb][:, 128:256], axis=AX, op=OP.max, negate=True)
                P.step("vector", rd, reads=[R.gPB[mb]], writes=[ggn])
                P.step("tensor", (lambda e, mb=mb, ngc=ngc: e.matmul(
                    R.PB[mb][0:2, 256:384], ngc[:, 0:2], R.ident[:], start=True, stop=True)),
                       reads=[ggn, R.gconst], writes=[R.gPB[mb]])
                P.step("vector", (lambda e, mb=mb, nmA=nmA: e.tensor_copy(nmA[0:2, :], R.PB[mb][0:2, 256:384])),
                       reads=[R.gPB[mb]], writes=[ggm])
                def cmm(e, rho0=rho0, sbk=sbk, nmA=nmA):
                    last = None
                    for rho in range(rho0, 4):
                        for cc in range(2):
                            co = (rho % 2) * 256 + cc * 128
                            last = e.matmul(R.PB[sbk[rho // 2]][:, co: co + 128],
                                            R.ec2[0:2, cc * 128:(cc + 1) * 128], nmA[0:2, :],
                                            start=False, stop=(cc == 1 and rho % 2 == 1))
                    return last
                P.step("tensor", cmm, reads=[ggm, R.gconst], writes=wr)
                def ex(e, rho0=rho0, sbk=sbk, Eap=Eap):
                    last = None
                    for bi in range(rho0 // 2, 2):
                        last = e.activation(Eap[:, bi * 512:(bi + 1) * 512],
                                            R.PB[sbk[bi]][:, 0:512], AF.Exp, scale=R.c8[:])
                    return last
                P.step("scalar", ex, reads=wr + [R.gc], writes=gEi)
                def av(e, w=w, rho0=rho0, h=h, ab=ab, Eap=Eap):
                    last = None
                    for rho in range(rho0, 4):
                        jw = w - 1 + rho // 2
                        vti = (jw % 3) * 2 + (rho % 2)
                        vap = vext[:, vti * 1040 + h * 65: vti * 1040 + h * 65 + 65]
                        last = e.matmul(R.PB[ab][0:65, 0:256], vap, Eap[:, rho * WIN:(rho + 1) * WIN],
                                        start=(rho == rho0), stop=(rho == 3))
                    return last
                P.step("tensor", av, reads=[gvx] + gEi, writes=[R.gPB[ab]])
                P.step("scalar", (lambda e, ab=ab, lnr=lnr: e.activation(
                    lnr[0:1, 0:256], R.PB[ab][64:65, 0:256], AF.Ln)),
                       reads=[R.gPB[ab]], writes=[grecip])
                P.step("scalar", (lambda e, wi=wi, lnr=lnr: e.activation(
                    reciprow[0:1, wi * 256:(wi + 1) * 256],
                    lnr[0:1, 0:256], AF.Exp, scale=R.cm1[0:1, :])),
                       reads=[R.gc], writes=[grecip])
                P.step("vector", (lambda e, dt_=dt_, prow=prow, wi=wi, ab=ab: e.tensor_copy(
                    qav[prow:prow + 64, dt_ * 512 + wi * 256: dt_ * 512 + wi * 256 + 256],
                    R.PB[ab][0:64, 0:256])),
                    reads=[R.gPB[ab]], writes=[gQ[dt_][hf][wi]])
            nb = 4 + 2 * (h % 2)
            ho = R.halfones0 if hf == 0 else R.halfones1
            P.step("tensor", (lambda e, ho=ho, nb=nb: e.matmul(R.PB[nb][:, 0:512], ho[0:1, :],
                                                               reciprow[0:1, :], start=True, stop=True)),
                   reads=[grecip, R.gconst], writes=[R.gPB[nb]])
            P.step("vector", (lambda e, dt_=dt_, prow=prow, nb=nb: e.tensor_tensor(
                qav[prow:prow + 64, dt_ * 512:(dt_ + 1) * 512],
                qav[prow:prow + 64, dt_ * 512:(dt_ + 1) * 512],
                R.PB[nb][prow:prow + 64, 0:512], OP.mult)),
                reads=[R.gPB[nb]], writes=gQ[dt_][hf])
        for dto in range(DT):
            src = bass.AP(wo.tensor, l * 1024 * 1024 + dto * 128,
                          [[1024, 128], [128 * 1024, 8], [1, 128]])
            P.step("gpsimd", (lambda e, src=src: e.dma_start(out=woB, in_=src)),
                   writes=[gxsbf], dma="b_wo")
            bk = 4 + dto % 2
            def womm(e, bk=bk):
                last = None
                for ft in range(DT):
                    last = e.matmul(R.PB[bk][:], woB[:, ft * 128:(ft + 1) * 128],
                                    qav[:, ft * 512:(ft + 1) * 512],
                                    start=(ft == 0), stop=(ft == DT - 1))
                return last
            P.step("tensor", womm, reads=[gxsbf] + [g for pair in gQ for sub in pair for g in sub],
                   writes=[R.gPB[bk]])
            P.step("vector", (lambda e, bk=bk, dto=dto, c0=c0: e.tensor_tensor(
                R.xT(s, dto)[:, c0:c0 + 512], R.PB[bk][:], R.xT(s, dto)[:, c0:c0 + 512], OP.add)),
                reads=[R.gPB[bk]], writes=[R.gXT])

def emit_b_ffn(P, R, dram, s, l):
    R.new_section(P)
    ln = LNArea(P, R)
    w1 = dram["b_w1"]
    w2 = dram["b_w2"]
    xnbf = R.carve2(8 * 512)
    zsh = R.carve2(8 * 512)
    gzsh = Guard()
    w1a = R.carve2(8 * 128)
    w1g = R.carve2(8 * 128)
    Y1 = R.carve2(22 * 512)
    abuf = R.carve2(512)
    gbuf = R.carve2(512)
    w2b = R.carve2(22 * 128)
    lgb = R.carve4(16)
    b1c = R.carve4(44)
    b2c = R.carve4(8)
    bcol = R.carve4(16)
    gbc = Guard()
    gxn, gw1a, gw1g, gY1, gab, ggb_, gw2, glgb, gb1, gb2 = (Guard() for _ in range(10))
    _save_bcols(P, R, s, bcol, gbc)
    P.step("sync", (lambda e: e.dma_start(out=lgb[:, 0:8], in_=dram["b_lng"][l])), writes=[glgb], dma="b_lgb")
    P.step("sync", (lambda e: e.dma_start(out=lgb[:, 8:16], in_=dram["b_lnb"][l])), writes=[glgb], dma="b_lgb")
    P.step("sync", (lambda e: e.dma_start(out=b1c[:, 0:22], in_=dram["b_b1a"][l])), writes=[gb1], dma="b_b1")
    P.step("sync", (lambda e: e.dma_start(out=b1c[:, 22:44], in_=dram["b_b1g"][l])), writes=[gb1], dma="b_b1")
    P.step("sync", (lambda e: e.dma_start(out=b2c[:], in_=dram["b_b2"][l])), writes=[gb2], dma="b_b1")
    gamma = [lgb[:, d:d + 1] for d in range(8)]
    beta = [lgb[:, 8 + d:9 + d] for d in range(8)]
    NT1 = 22

    for icg in range(4):
        c0 = icg * 512
        plans = [_shift_plan(R, s, d, c0, 512, bcol, gbc) for d in range(DT)]
        out_aps = [xnbf[:, d * 512:(d + 1) * 512] for d in range(DT)]
        emit_ln_shifted(P, R, ln, plans, zsh, gzsh, out_aps, gxn, gamma, beta, glgb)
        if DBG and s == 0 and l == 0 and icg == 1:
            gd5 = Guard()
            P.step("gpsimd", (lambda e: e.dma_start(out=dram["dbg"][:, 0:4096], in_=xnbf[:])),
                   reads=[gxn], writes=[gd5], dma="dbg3")
            P.step("gpsimd", (lambda e: e.dma_start(out=dram["dbg"][:, 4096:4112], in_=bcol[:])),
                   reads=[gbc], writes=[gd5], dma="dbg3")
        for ft in range(NT1):
            mw = 128 if ft < 21 else 42
            for half, (wbuf, gwb) in enumerate(((w1a, gw1a), (w1g, gw1g))):
                off = half * 2730 + ft * 128
                src = bass.AP(w1.tensor, l * 1024 * 5460 + off,
                              [[5460, 128], [128 * 5460, 8], [1, mw]])
                dst = bass.AP(wbuf.tensor, wbuf.offset, [[wbuf.tensor.shape[1], 128], [128, 8], [1, mw]])
                P.step("gpsimd", (lambda e, dst=dst, src=src: e.dma_start(out=dst, in_=src)),
                       writes=[gwb], dma=f"b_w1{half}")
                bk = half
                def w1mm(e, wbuf=wbuf, mw=mw, bk=bk):
                    last = None
                    for d in range(DT):
                        last = e.matmul(R.PB[bk][0:mw, :], wbuf[:, d * 128: d * 128 + mw],
                                        xnbf[:, d * 512:(d + 1) * 512],
                                        start=(d == 0), stop=(d == DT - 1))
                    return last
                P.step("tensor", w1mm, reads=[gwb, gxn], writes=[R.gPB[bk]])
            P.step("scalar", (lambda e, mw=mw, ft=ft: e.activation(
                abuf.bitcast(BF16)[0:mw, 0:512], R.PB[0][0:mw, :], AF.Identity, bias=b1c[0:mw, ft:ft + 1])),
                reads=[R.gPB[0], gb1], writes=[gab])
            P.step("scalar", (lambda e, mw=mw, ft=ft: e.activation(
                gbuf.bitcast(BF16)[0:mw, 0:512], R.PB[1][0:mw, :], AF.Gelu, bias=b1c[0:mw, 22 + ft:23 + ft])),
                reads=[R.gPB[1], gb1], writes=[ggb_])
            P.step("vector", (lambda e, mw=mw, ft=ft: e.tensor_tensor(
                Y1[0:mw, ft * 512:(ft + 1) * 512], abuf.bitcast(BF16)[0:mw, 0:512],
                gbuf.bitcast(BF16)[0:mw, 0:512], OP.mult)),
                reads=[gab, ggb_], writes=[gY1])
        for dto in range(DT):
            src = bass.AP(w2.tensor, l * 2730 * 1024 + dto * 128,
                          [[1024, 128], [128 * 1024, 21], [1, 128]])
            P.step("gpsimd", (lambda e, src=src: e.dma_start(out=w2b[:, 0:21 * 128], in_=src)),
                   writes=[gw2], dma="b_w2")
            src2 = bass.AP(w2.tensor, l * 2730 * 1024 + 21 * 128 * 1024 + dto * 128,
                           [[1024, 42], [1, 128]])
            P.step("gpsimd", (lambda e, src2=src2: e.dma_start(out=w2b[0:42, 21 * 128:22 * 128], in_=src2)),
                   writes=[gw2], dma="b_w2")
            bk = 2 + dto % 2
            def w2mm(e, bk=bk):
                last = None
                for ft in range(NT1):
                    mw = 128 if ft < 21 else 42
                    last = e.matmul(R.PB[bk][:], w2b[0:mw, ft * 128:(ft + 1) * 128],
                                    Y1[0:mw, ft * 512:(ft + 1) * 512],
                                    start=(ft == 0), stop=(ft == NT1 - 1))
                return last
            P.step("tensor", w2mm, reads=[gw2, gY1], writes=[R.gPB[bk]])
            P.step("vector", (lambda e, bk=bk, dto=dto, c0=c0: e.tensor_tensor(
                R.xT(s, dto)[:, c0:c0 + 512], R.PB[bk][:], R.xT(s, dto)[:, c0:c0 + 512], OP.add)),
                reads=[R.gPB[bk]], writes=[R.gXT])

def emit_ln_shifted(P, R, ln, plans, zsh, gzsh, out_aps, gout, gamma, beta, ggb):
    for d in range(DT):
        _copy_shifted(P, R, zsh[:, d * 512:(d + 1) * 512], plans[d], gzsh)
    for d in range(DT):
        b = d % 2
        P.step("vector", (lambda e, d=d, b=b: e.tensor_tensor(
            ln.zsq[b][:], zsh[:, d * 512:(d + 1) * 512], zsh[:, d * 512:(d + 1) * 512], OP.mult)),
            reads=[gzsh], writes=[ln.gzsq[b]])
        def statmm(e, d=d, b=b):
            e.matmul(R.PB[6][0:1, :], R.ones_col[:], zsh[:, d * 512:(d + 1) * 512],
                     start=(d == 0), stop=(d == DT - 1))
            return e.matmul(R.PB[7][0:1, :], R.ones_col[:], ln.zsq[b][:],
                            start=(d == 0), stop=(d == DT - 1))
        P.step("tensor", statmm, reads=[gzsh, ln.gzsq[b], R.gconst], writes=[R.gPB[6], R.gPB[7]])
    P.step("vector", (lambda e: e.tensor_single_scalar(ln.row(0), R.PB[6][0:1, :], 1.0 / D, OP.mult)),
           reads=[R.gPB[6]], writes=[ln.grows])
    P.step("vector", (lambda e: e.tensor_single_scalar(ln.row(1), R.PB[7][0:1, :], 1.0 / D, OP.mult)),
           reads=[R.gPB[7]], writes=[ln.grows])
    P.step("vector", (lambda e: e.tensor_tensor(ln.row(3), ln.row(0), ln.row(0), OP.mult)),
           reads=[ln.grows], writes=[ln.grows])
    P.step("vector", (lambda e: e.tensor_tensor(ln.row(1), ln.row(1), ln.row(3), OP.subtract)),
           reads=[ln.grows], writes=[ln.grows])
    P.step("scalar", (lambda e: e.activation(ln.row(2), ln.row(1), AF.Ln, bias=ln.eps[0:1, :])),
           reads=[ln.grows, ln.gc], writes=[ln.grows])
    P.step("scalar", (lambda e: e.activation(ln.row(2), ln.row(2), AF.Exp, scale=ln.mhalf[0:1, :])),
           reads=[ln.gc], writes=[ln.grows])
    P.step("vector", (lambda e: e.tensor_tensor(ln.row(3), ln.row(0), ln.row(2), OP.mult)),
           reads=[ln.grows], writes=[ln.grows])
    P.step("tensor", (lambda e: e.matmul(R.PB[6][:, 0:LCH], R.ones_row[0:1, :], ln.row(2),
                                         start=True, stop=True)),
           reads=[ln.grows, R.gconst], writes=[R.gPB[6]])
    P.step("tensor", (lambda e: e.matmul(R.PB[7][:, 0:LCH], R.ones_row[0:1, :], ln.row(3),
                                         start=True, stop=True)),
           reads=[ln.grows, R.gconst], writes=[R.gPB[7]])
    P.step("vector", (lambda e: e.tensor_copy(ln.rstdB[:], R.PB[6][:, 0:LCH])), reads=[R.gPB[6]], writes=[ln.gB])
    P.step("vector", (lambda e: e.tensor_copy(ln.msB[:], R.PB[7][:, 0:LCH])), reads=[R.gPB[7]], writes=[ln.gB])
    for d in range(DT):
        P.step("vector", (lambda e, d=d: e.tensor_tensor(out_aps[d], zsh[:, d * 512:(d + 1) * 512],
                                                         ln.rstdB[:], OP.mult)),
               reads=[gzsh, ln.gB], writes=[gout])
        P.step("vector", (lambda e, d=d: e.tensor_tensor(out_aps[d], out_aps[d], ln.msB[:], OP.subtract)),
               reads=[ln.gB], writes=[gout])
        P.step("vector", (lambda e, d=d: e.tensor_scalar(out_aps[d], out_aps[d], gamma[d], beta[d],
                                                         OP.mult, OP.add)),
               reads=[ggb], writes=[gout])

def emit_phase_b(P, R, dram, nlayers=6, final=True):
    xsnap = dram["b_xsnap"]
    for s in range(NSEQ):
        P.step("sync", (lambda e, s=s: e.dma_start(
            out=bass.AP(xsnap.tensor, s * DT * T * 128, [[DT * T, 128], [1, DT * T]]),
            in_=R.XT[:, s * DT * T:(s + 1) * DT * T])),
            reads=[R.gXT], dma="b_snap")
    snap_tok = (P.dsem("b_snap"), P.dsem("b_snap").n)
    for s in range(NSEQ):
        for l in range(nlayers):
            emit_b_attn(P, R, dram, s, l)
            emit_b_ffn(P, R, dram, s, l)
    if final:
        R.new_section(P)
        ln = LNArea(P, R)
        snapb = [R.carve4(512) for _ in range(2)]
        gsn = [Guard(), Guard()]
        for s in range(NSEQ):
            for cidx in range(4):
                c0 = cidx * 512
                for d in range(DT):
                    b = d % 2
                    src = bass.AP(xsnap.tensor, s * 128 * DT * T + d * T + c0,
                                  [[DT * T, 128], [1, 512]])
                    P.step("sync", (lambda e, b=b, src=src: e.dma_start(out=snapb[b][:], in_=src)),
                           extra_waits=[snap_tok], writes=[gsn[b]], dma=f"b_snapb{b}")
                    P.step("vector", (lambda e, b=b, d=d, s=s, c0=c0: e.tensor_tensor(
                        R.xT(s, d)[:, c0:c0 + 512], R.xT(s, d)[:, c0:c0 + 512], snapb[b][:], OP.add)),
                        reads=[gsn[b]], writes=[R.gXT])
                src_aps = [R.xT(s, d)[:, c0:c0 + 512] for d in range(DT)]
                emit_ln_chunk(P, R, ln, src_aps, R.gXT, src_aps, R.gXT, pst_banks=(7, 4), bc_banks=(5, 6))



def emit_c_table(P, R, dram):
    R.new_section(P)
    resb = R.carve2(48)
    ohb = R.carve2(512)
    g2row = R.carve2(512)
    gre, goh, gg2 = Guard(), Guard(), Guard()
    for kt in range(3):
        kk = 64 if kt == 2 else 128
        P.step("gpsimd", (lambda e, kt=kt, kk=kk: e.dma_start(
            out=resb[0:kk, kt * 16:(kt + 1) * 16], in_=dram["c_rel"][kt * 128: kt * 128 + kk, :])),
            writes=[gre], dma="c_tabr")
    for ch in range(8):
        cw = 511 if ch == 7 else 512
        for kt in range(3):
            kk = 64 if kt == 2 else 128
            P.step("gpsimd", (lambda e, kt=kt, kk=kk, ch=ch, cw=cw: e.dma_start(
                out=ohb[0:kk, 0:cw], in_=dram["c_bucket_oh"][kt * 128: kt * 128 + kk, ch * 512: ch * 512 + cw])),
                writes=[goh], dma="c_taboh")
            P.step("tensor", (lambda e, kt=kt, kk=kk, cw=cw: e.matmul(
                R.PB[0][0:16, 0:cw], resb[0:kk, kt * 16:(kt + 1) * 16], ohb[0:kk, 0:cw],
                start=(kt == 0), stop=(kt == 2))),
                reads=[gre, goh], writes=[R.gPB[0]])
        P.step("vector", (lambda e, cw=cw: e.tensor_copy(g2row[0:16, 0:cw], R.PB[0][0:16, 0:cw])),
               reads=[R.gPB[0]], writes=[gg2])
        P.step("gpsimd", (lambda e, ch=ch, cw=cw: e.dma_start(
            out=dram["c_g2"][:, ch * 512: ch * 512 + cw], in_=g2row[0:16, 0:cw])),
            reads=[gg2], dma="c_tabg2")

def emit_phase_c(P, R, dram):
    emit_c_table(P, R, dram)
    g2 = dram["c_g2"]
    for s in range(NSEQ):
        R.new_section(P)
        xcf = R.carve2(8 * T)
        gateo = R.carve2(T)
        U = R.carve2(2 * 512)
        qk = R.carve2(T)
        vh = R.carve2(16 * 65)
        Eb = R.carve2(512)
        runmax = R.carve2(512)
        avTt = R.carve2(512)
        avsb = R.carve2(512)
        gateb = R.carve2(512)
        gg1 = Eb
        gg2b = avsb
        negc4 = R.carve2(16)
        nmT = R.carve2(128)
        galb = R.carve2(128)
        gblb = R.carve2(128)
        w2gb = R.carve2(16)
        gruwb = R.carve2(64)
        gselb = R.carve2(16)
        grubb = R.carve2(16)
        bvrow = R.carve2(1024)
        wqk = R.carve2(1024)
        wob = wqk
        Jsb = R.carve2(128)
        ec4sb = R.carve2(512)
        eh16sb = R.carve2(2048)
        avrd = xcf[:, 0:4096]
        b2gr = R.carve4(16)
        reciprow = R.carve4(512)
        bqkc = R.carve4(16)
        gxcf, ggo, gU, gqk, gvh, gE, grm, gavt, gavs, ggb = (Guard() for _ in range(10))
        ggg, gnc, gnm, ggal, gw2g, ggw, ggs, ggbb = (Guard() for _ in range(8))
        grec, gbq, gbv, gwqk, gwob = (Guard() for _ in range(5))
        ln = LNArea(P, R, a2alias=((Eb, gE), (avTt, gavt), (avsb, gavs), (runmax, grm)))

        for ic in range(4):
            for d in range(DT):
                P.step("vector", (lambda e, d=d, ic=ic, s=s: e.tensor_copy(
                    xcf[:, d * T + ic * 512: d * T + ic * 512 + 512],
                    R.xT(s, d)[:, ic * 512:(ic + 1) * 512])),
                    reads=[R.gXT], writes=[gxcf])
        P.step("gpsimd", (lambda e: e.dma_start(out=bvrow[0:1, :], in_=dram["c_bvrow"])),
               writes=[gbv], dma="c_bv")
        gJc = Guard()
        P.step("gpsimd", (lambda e: e.dma_start(out=Jsb[:], in_=dram["c_J"])), writes=[gJc], dma="c_Jc")
        P.step("gpsimd", (lambda e: e.dma_start(out=ec4sb[0:4, :], in_=dram["c_ec4"])), writes=[gJc], dma="c_Jc")
        P.step("gpsimd", (lambda e: e.dma_start(out=eh16sb[0:16, :], in_=dram["c_eh16cat"])), writes=[gJc], dma="c_Jc")
        P.step("gpsimd", (lambda e: e.dma_start(out=gruwb[0:8, 0:64], in_=dram["c_gruw"])),
               writes=[ggw], dma="c_gw")
        P.step("gpsimd", (lambda e: e.dma_start(out=gselb[0:8, 0:2], in_=dram["c_gsel"])),
               writes=[ggs], dma="c_gs")
        P.step("gpsimd", (lambda e: e.dma_start(out=grubb[0:8, 0:1], in_=dram["c_grub"])),
               writes=[ggbb], dma="c_gbb")
        P.step("gpsimd", (lambda e: e.dma_start(out=b2gr[0:16, 2:3], in_=dram["c_gruc"])),
               writes=[ggbb], dma="c_gbb")
        P.step("tensor", (lambda e: e.matmul(R.PB[2][0:64, 0:2], gruwb[0:8, 0:64], gselb[0:8, 0:2],
                                             start=True, stop=True)),
               reads=[ggw, ggs], writes=[R.gPB[2]])
        P.step("vector", (lambda e: e.tensor_copy(w2gb[0:64, 0:2], R.PB[2][0:64, 0:2])),
               reads=[R.gPB[2]], writes=[gw2g])
        P.step("tensor", (lambda e: e.matmul(R.PB[3][0:1, 0:2], grubb[0:8, 0:1], gselb[0:8, 0:2],
                                             start=True, stop=True)),
               reads=[ggbb, ggs], writes=[R.gPB[3]])
        P.step("vector", (lambda e: e.tensor_copy(b2gr[0:1, 0:2], R.PB[3][0:1, 0:2])),
               reads=[R.gPB[3]], writes=[ggg])
        P.step("tensor", (lambda e: e.matmul(R.PB[3][0:16, 0:2], R.ones_row[0:1, 0:16],
                                             b2gr[0:1, 0:2], start=True, stop=True)),
               reads=[ggg, R.gconst], writes=[R.gPB[3]])
        P.step("vector", (lambda e: e.tensor_copy(b2gr[0:16, 0:2], R.PB[3][0:16, 0:2])),
               reads=[R.gPB[3]], writes=[ggg])
        def galbuild(e):
            last = None
            for t in range(DT):
                e.memset(galb[:, t * 16:(t + 1) * 16], 0.0)
                e.memset(gblb[:, t * 16:(t + 1) * 16], 0.0)
                e.tensor_copy(galb[0:64, t * 16 + 2 * t: t * 16 + 2 * t + 1], w2gb[0:64, 0:1])
                e.tensor_copy(galb[64:128, t * 16 + 2 * t + 1: t * 16 + 2 * t + 2], w2gb[0:64, 0:1])
                e.tensor_copy(gblb[0:64, t * 16 + 2 * t: t * 16 + 2 * t + 1], w2gb[0:64, 1:2])
                last = e.tensor_copy(gblb[64:128, t * 16 + 2 * t + 1: t * 16 + 2 * t + 2], w2gb[0:64, 1:2])
            return last
        P.step("vector", galbuild, reads=[gw2g], writes=[ggal])
        for ic in range(4):
            def gamm(e, ic=ic):
                last = None
                for t in range(DT):
                    e.matmul(R.PB[2][0:16, :], galb[:, t * 16:(t + 1) * 16],
                             xcf[:, t * T + ic * 512: t * T + ic * 512 + 512],
                             start=(t == 0), stop=(t == DT - 1))
                for t in range(DT):
                    last = e.matmul(R.PB[3][0:16, :], gblb[:, t * 16:(t + 1) * 16],
                                    xcf[:, t * T + ic * 512: t * T + ic * 512 + 512],
                                    start=(t == 0), stop=(t == DT - 1))
                return last
            P.step("tensor", gamm, reads=[ggal, gxcf], writes=[R.gPB[2], R.gPB[3]])
            P.step("scalar", (lambda e: e.activation(gg1[0:16, 0:512], R.PB[2][0:16, :], AF.Sigmoid,
                                                     bias=b2gr[0:16, 0:1])),
                   reads=[R.gPB[2], ggg], writes=[ggg, gE])
            P.step("scalar", (lambda e: e.activation(gg2b[0:16, 0:512], R.PB[3][0:16, :], AF.Sigmoid,
                                                     bias=b2gr[0:16, 1:2])),
                   reads=[R.gPB[3], ggg], writes=[ggg, gavs])
            P.step("vector", (lambda e: e.tensor_scalar(gg2b[0:16, 0:512], gg2b[0:16, 0:512],
                                                        b2gr[0:16, 2:3], -1.0, OP.mult, OP.add)),
                   reads=[ggbb], writes=[ggg])
            P.step("vector", (lambda e: e.tensor_tensor(gg1[0:16, 0:512], gg1[0:16, 0:512],
                                                        gg2b[0:16, 0:512], OP.mult)),
                   writes=[ggg])
            P.step("vector", (lambda e, ic=ic: e.tensor_single_scalar(
                gg1[0:16, 0:512], gg1[0:16, 0:512], 8.0, OP.mult)),
                reads=[ggg], writes=[ggg])
            P.step("vector", (lambda e, ic=ic: e.tensor_scalar(gateo[0:16, ic * 512:(ic + 1) * 512],
                                                               gg1[0:16, 0:512], 16.0, None, OP.add)),
                   reads=[ggg], writes=[ggo])
        if DBG and s == 0:
            gdC = Guard()
            P.step("gpsimd", (lambda e: e.dma_start(out=dram["dbg"][0:16, 0:2048], in_=gateo[0:16, :])),
                   reads=[ggo], writes=[gdC], dma="dbg4")
            P.step("gpsimd", (lambda e: e.dma_start(out=dram["dbg"][:, 8720:12816], in_=xcf[:, 0:4096])),
                   reads=[gxcf], writes=[gdC], dma="dbg4")
            P.step("gpsimd", (lambda e: e.dma_start(out=dram["dbg"][:, 12816:16912], in_=xcf[:, 4096:8192])),
                   reads=[gxcf], writes=[gdC], dma="dbg4")
        for h in range(16):
            dt_, hf = h // 2, h % 2
            P.step("gpsimd", (lambda e, h=h: e.dma_start(
                out=wqk[:, 0:512], in_=bass.AP(dram["c_wvT"].tensor, h * 64,
                                               [[1024, 128], [128 * 1024, 8], [1, 64]]))),
                writes=[gwqk], dma="c_wqk")
            for tt in range(16):
                bk = tt % 2
                def vmm(e, tt=tt, bk=bk, h=h):
                    for d in range(DT):
                        e.matmul(R.PB[bk][:, 0:64],
                                 xcf[:, d * T + tt * 128: d * T + tt * 128 + 128],
                                 wqk[:, d * 64:(d + 1) * 64],
                                 start=(d == 0), stop=False)
                    return e.matmul(R.PB[bk][:, 0:64], R.ones_row_bf[0:1, :],
                                    bvrow[0:1, h * 64:(h + 1) * 64], start=False, stop=True)
                P.step("tensor", vmm, reads=[gwqk, gxcf, gbv, R.gconst], writes=[R.gPB[bk]])
                P.step("scalar", (lambda e, tt=tt, bk=bk: e.copy(
                    vh[:, tt * 65: tt * 65 + 64], R.PB[bk][:, 0:64])),
                    reads=[R.gPB[bk]], writes=[gvh])
            P.step("vector", (lambda e: e.memset(vh[:].rearrange("p (a b) -> p a b", b=65)[:, :, 64:65], 1.0)),
                   writes=[gvh])
            P.step("gpsimd", (lambda e, h=h: e.dma_start(
                out=wqk[:], in_=bass.AP(dram["c_wqkT"].tensor, h * 128,
                                        [[2048, 128], [128 * 2048, 8], [1, 128]]))),
                writes=[gwqk], dma="c_wqk")
            P.step("gpsimd", (lambda e, h=h: e.dma_start(out=bqkc[64:128, 0:2],
                                                         in_=dram["c_bqk"][h])),
                writes=[gbq], dma="c_bqk")
            for ic in range(4):
                def qkmm(e, ic=ic):
                    last = None
                    for d in range(DT):
                        e.matmul(R.PB[6][64:128, :], wqk[:, d * 128: d * 128 + 64],
                                 xcf[:, d * T + ic * 512: d * T + ic * 512 + 512],
                                 start=(d == 0), stop=(d == DT - 1))
                    for d in range(DT):
                        last = e.matmul(R.PB[7][64:128, :], wqk[:, d * 128 + 64: d * 128 + 128],
                                        xcf[:, d * T + ic * 512: d * T + ic * 512 + 512],
                                        start=(d == 0), stop=(d == DT - 1))
                    return last
                P.step("tensor", qkmm, reads=[gwqk, gxcf], writes=[R.gPB[6], R.gPB[7]])
                P.step("scalar", (lambda e, ic=ic: e.activation(
                    eh16sb[64:128, ic * 512:(ic + 1) * 512], R.PB[6][64:128, :], AF.Identity,
                    bias=bqkc[64:128, 0:1])),
                    reads=[R.gPB[6], gbq], writes=[gqk])
                P.step("scalar", (lambda e, ic=ic: e.activation(
                    qk[64:128, ic * 512:(ic + 1) * 512], R.PB[7][64:128, :], AF.Identity,
                    bias=bqkc[64:128, 1:2])),
                    reads=[R.gPB[7], gbq], writes=[gqk])
            if DBG and s == 0 and h == 0:
                gdC2 = Guard()
                P.step("gpsimd", (lambda e: e.dma_start(out=dram["dbg"][:, 2048:4096], in_=eh16sb[:, 0:2048])),
                       reads=[gqk], writes=[gdC2], dma="dbg5")
                P.step("gpsimd", (lambda e: e.dma_start(out=dram["dbg"][:, 4096:6144], in_=qk[:, 0:2048])),
                       reads=[gqk], writes=[gdC2], dma="dbg5")
                P.step("gpsimd", (lambda e: e.dma_start(out=dram["dbg"][:, 6144:7184], in_=vh[:, 0:1040])),
                       reads=[gvh], writes=[gdC2], dma="dbg5")
                P.step("gpsimd", (lambda e: e.dma_start(out=dram["dbg"][:, 16912:17936], in_=wqk[:, 0:1024])),
                       reads=[gwqk], writes=[gdC2], dma="dbg5")
            for ic in range(4):
                P.step("tensor", (lambda e, h=h, ic=ic: e.matmul(
                    R.PB[7][:, 0:512], eh16sb[0:16, h * 128:(h + 1) * 128],
                    gateo[0:16, ic * 512:(ic + 1) * 512], start=True, stop=True)),
                    reads=[ggo, gJc], writes=[R.gPB[7]])
                P.step("vector", (lambda e: e.tensor_copy(gateb[:, 0:512], R.PB[7][:, 0:512])),
                       reads=[R.gPB[7]], writes=[ggb])
                for p2 in range(2):
                    for rr in range(8):
                        P.step("sync", (lambda e, h=h, ic=ic, rr=rr: e.dma_start(
                            out=U[:], in_=bass.AP(g2.tensor, h * 4095 + 512 * ic + 256 * rr,
                                                  [[1, 128], [128, 2], [1, 512]]))),
                            writes=[gU], dma="c_U")
                        for pp in range(2):
                            r = rr * 2 + pp
                            jt = 15 - r
                            P.step("vector", (lambda e, pp=pp: e.tensor_tensor(
                                U[:, pp * 512:(pp + 1) * 512], U[:, pp * 512:(pp + 1) * 512],
                                gateb[:, 0:512], OP.mult)),
                                reads=[ggb], writes=[gU])
                            bk = r % 2
                            def simj(e, jt=jt, pp=pp, bk=bk, ic=ic, p2=p2):
                                e.matmul(R.PB[bk][:], qk[64:128, jt * 128:(jt + 1) * 128],
                                         eh16sb[64:128, ic * 512:(ic + 1) * 512], start=True, stop=False)
                                last = e.matmul(R.PB[bk][:], Jsb[:], U[:, pp * 512:(pp + 1) * 512],
                                                start=False, stop=(p2 == 0))
                                if p2 == 1:
                                    for cc in range(4):
                                        last = e.matmul(R.PB[bk][:, cc * 128:(cc + 1) * 128],
                                                        ec4sb[0:4, cc * 128:(cc + 1) * 128],
                                                        nmT[0:4, 0:128], start=False, stop=(cc == 3))
                                return last
                            rds = [gqk, gU, gJc] + ([gnm] if p2 == 1 else [])
                            P.step("tensor", simj, reads=rds, writes=[R.gPB[bk]])
                            if p2 == 0:
                                if r == 0:
                                    P.step("vector", (lambda e, bk=bk: e.tensor_copy(
                                        runmax[:, 0:512], R.PB[bk][:])),
                                        reads=[R.gPB[bk]], writes=[grm])
                                else:
                                    P.step("vector", (lambda e, bk=bk: e.tensor_tensor(
                                        runmax[:, 0:512], R.PB[bk][:], runmax[:, 0:512], OP.max)),
                                        reads=[R.gPB[bk]], writes=[grm])
                            else:
                                P.step("scalar", (lambda e, bk=bk: e.activation(
                                    Eb[:, 0:512], R.PB[bk][:], AF.Exp, scale=R.c0125[:])),
                                    reads=[R.gPB[bk], R.gc], writes=[gE])
                                def avmm(e, jt=jt, r=r):
                                    return e.matmul(R.PB[4][0:65, :], vh[:, jt * 65: jt * 65 + 65],
                                                    Eb[:, 0:512], start=(r == 0), stop=(r == 15))
                                P.step("tensor", avmm, reads=[gvh, gE], writes=[R.gPB[4]])
                                if DBG and s == 0 and h == 0 and ic == 0 and r == 15:
                                    gdC4 = Guard()
                                    P.step("gpsimd", (lambda e: e.dma_start(
                                        out=dram["dbg"][:, 8208:8720], in_=Eb[:, 0:512])),
                                        reads=[gE], writes=[gdC4], dma="dbg6")
                    if DBG and s == 0 and h == 0 and ic == 0 and p2 == 0:
                        gdC3 = Guard()
                        P.step("gpsimd", (lambda e: e.dma_start(out=dram["dbg"][:, 7184:7696],
                                                                in_=runmax[:, 0:512])),
                               reads=[grm], writes=[gdC3], dma="dbg6")
                        P.step("gpsimd", (lambda e: e.dma_start(out=dram["dbg"][:, 7696:8208],
                                                                in_=gateb[:, 0:512])),
                               reads=[ggb], writes=[gdC3], dma="dbg6")
                    if p2 == 0:
                        def tpmm(e):
                            last = None
                            for cc in range(4):
                                last = e.matmul(R.PB[2][:, cc * 128:(cc + 1) * 128],
                                                runmax[:, cc * 128:(cc + 1) * 128], R.ident[:],
                                                start=True, stop=True)
                            return last
                        P.step("tensor", tpmm, reads=[grm, R.gconst], writes=[R.gPB[2]])
                        def rdc(e):
                            last = None
                            for cc in range(4):
                                last = e.tensor_reduce(negc4[:, cc:cc + 1],
                                                       R.PB[2][:, cc * 128:(cc + 1) * 128],
                                                       axis=AX, op=OP.max, negate=True)
                            return last
                        P.step("vector", rdc, reads=[R.gPB[2]], writes=[gnc])
                        P.step("tensor", (lambda e: e.matmul(R.PB[3][0:4, 0:128], negc4[:, 0:4],
                                                             R.ident[:], start=True, stop=True)),
                               reads=[gnc, R.gconst], writes=[R.gPB[3]])
                        P.step("vector", (lambda e: e.tensor_copy(nmT[0:4, 0:128], R.PB[3][0:4, 0:128])),
                               reads=[R.gPB[3]], writes=[gnm])
                P.step("scalar", (lambda e: e.activation(reciprow[0:1, 0:512], R.PB[4][64:65, 0:512],
                                                         AF.Ln)),
                       reads=[R.gPB[4]], writes=[grec])
                P.step("scalar", (lambda e: e.activation(reciprow[0:1, 0:512], reciprow[0:1, 0:512],
                                                         AF.Exp, scale=R.cm1[0:1, :])),
                       reads=[R.gc], writes=[grec])
                P.step("tensor", (lambda e: e.matmul(R.PB[5][:, 0:512], R.halfones0[0:1, :],
                                                     reciprow[0:1, 0:512], start=True, stop=True)),
                       reads=[grec, R.gconst], writes=[R.gPB[5]])
                P.step("scalar", (lambda e: e.copy(avsb[0:64, 0:512], R.PB[4][0:64, :])),
                       reads=[R.gPB[4]], writes=[gavs])
                P.step("vector", (lambda e: e.tensor_tensor(avTt[0:64, 0:512], avsb[0:64, 0:512],
                                                            R.PB[5][0:64, 0:512], OP.mult)),
                       reads=[gavs, R.gPB[5]], writes=[gavt])
                P.step("gpsimd", (lambda e, h=h, ic=ic: e.dma_start(
                    out=bass.AP(dram["c_avspill"].tensor, h * 64 * T + ic * 512, [[T, 64], [1, 512]]),
                    in_=avTt[0:64, 0:512])),
                    reads=[gavt], dma="c_spill")
        spill_tok = (P.dsem("c_spill"), P.dsem("c_spill").n)
        boT = R.carve4(8)
        gbo = Guard()
        P.step("sync", (lambda e: e.dma_start(out=boT[:, 0:8], in_=dram["c_boT"])),
               writes=[gbo], dma="c_bo")
        for ic in range(4):
            for ft in range(DT):
                P.step("sync", (lambda e, ft=ft, ic=ic: e.dma_start(
                    out=avrd[:, ft * 512:(ft + 1) * 512],
                    in_=bass.AP(dram["c_avspill"].tensor, ft * 128 * T + ic * 512, [[T, 128], [1, 512]]))),
                    extra_waits=[spill_tok], writes=[gxcf], dma="c_avrd")
            for dto in range(DT):
                P.step("gpsimd", (lambda e, dto=dto: e.dma_start(
                    out=wob, in_=bass.AP(dram["c_woT"].tensor, dto * 128,
                                         [[1024, 128], [128 * 1024, 8], [1, 128]]))),
                    writes=[gwqk], dma="c_wob")
                bk = 6 + dto % 2
                def womm(e, bk=bk):
                    last = None
                    for ft in range(DT):
                        last = e.matmul(R.PB[bk][:], wob[:, ft * 128:(ft + 1) * 128],
                                        avrd[:, ft * 512:(ft + 1) * 512],
                                        start=(ft == 0), stop=(ft == DT - 1))
                    return last
                P.step("tensor", womm, reads=[gwqk, gxcf], writes=[R.gPB[bk]])
                P.step("vector", (lambda e, bk=bk, dto=dto, ic=ic, s=s: e.scalar_tensor_tensor(
                    R.xT(s, dto)[:, ic * 512:(ic + 1) * 512], R.PB[bk][:],
                    boT[:, dto:dto + 1], R.xT(s, dto)[:, ic * 512:(ic + 1) * 512],
                    OP.add, OP.add)),
                    reads=[R.gPB[bk], gbo], writes=[R.gXT])
            src_aps = [R.xT(s, d)[:, ic * 512:(ic + 1) * 512] for d in range(DT)]
            emit_ln_chunk(P, R, ln, src_aps, R.gXT, src_aps, R.gXT, pst_banks=(7, 4), bc_banks=(5, 6))



def _emit_gamma_quant(P, R, dram, wname, qname, nrow, ncol):
    gbuf = R.carve4(1024)
    accc = R.carve4(48)
    tcols = R.carve4(16)
    onescol = R.carve4(16)
    mrow = R.carve4(128)
    qA = R.carve2(1024)
    qB = R.carve2(1024)
    gW, gacc, gq, gc1 = Guard(), Guard(), Guard(), Guard()
    nrc, ncc = nrow // 128, ncol // 1024
    chunks = [(r, c) for r in range(nrc) for c in range(ncc)]
    role = "d_" + wname
    P.step("vector", (lambda e: e.memset(onescol[:, 0:1], 1.0)), writes=[gc1])
    P.step("vector", (lambda e: e.memset(mrow[0:1, 0:128], -1.0)), writes=[gc1])
    P.step("vector", (lambda e: e.memset(accc[0:1, 40:41], 0.5 / (nrow * ncol))), writes=[gc1])
    P.step("vector", (lambda e: e.memset(accc[0:1, 41:42], 0.5e-5)), writes=[gc1])
    for i, (r, c) in enumerate(chunks):
        P.step("sync", (lambda e, r=r, c=c: e.dma_start(
            out=gbuf, in_=dram[wname][r * 128:(r + 1) * 128, c * 1024:(c + 1) * 1024])),
            writes=[gW], dma=role)
        P.step("vector", (lambda e, i=i: e.tensor_reduce(accc[:, i:i + 1], gbuf, axis=AX, op=OP.add,
                                                         apply_absolute_value=True)),
               reads=[gW], writes=[gacc])
    P.step("tensor", (lambda e: e.matmul(R.PB[0][0:1, 0:32], onescol[:, 0:1], accc[:, 0:32],
                                         start=True, stop=True)),
           reads=[gacc, gc1], writes=[R.gPB[0]])
    P.step("vector", (lambda e: e.tensor_reduce(accc[0:1, 32:33], R.PB[0][0:1, 0:32], axis=AX, op=OP.add)),
           reads=[R.gPB[0]], writes=[gacc])
    P.step("scalar", (lambda e: e.activation(accc[0:1, 33:34], accc[0:1, 32:33], AF.Identity,
                                             scale=accc[0:1, 40:41], bias=accc[0:1, 41:42])),
           reads=[gacc, gc1], writes=[gacc])
    def bcmm(e):
        e.matmul(R.PB[0][:, 0:1], R.ones_row[0:1, :], accc[0:1, 33:34], start=True, stop=True)
        return e.matmul(R.PB[1][:, 0:1], mrow[0:1, 0:128], accc[0:1, 33:34], start=True, stop=True)
    P.step("tensor", bcmm, reads=[gacc, gc1, R.gconst], writes=[R.gPB[0], R.gPB[1]])
    P.step("scalar", (lambda e: e.copy(tcols[:, 0:1], R.PB[0][:, 0:1])),
           reads=[R.gPB[0]], writes=[gacc])
    P.step("scalar", (lambda e: e.copy(tcols[:, 1:2], R.PB[1][:, 0:1])),
           reads=[R.gPB[1]], writes=[gacc])
    for (r, c) in chunks:
        P.step("sync", (lambda e, r=r, c=c: e.dma_start(
            out=gbuf, in_=dram[wname][r * 128:(r + 1) * 128, c * 1024:(c + 1) * 1024])),
            writes=[gW], dma=role)
        def quant(e):
            e.tensor_scalar(qA, gbuf, tcols[:, 0:1], None, OP.is_ge)
            e.tensor_scalar(qB, gbuf, tcols[:, 1:2], None, OP.is_le)
            return e.tensor_tensor(qA, qA, qB, OP.subtract)
        P.step("vector", quant, reads=[gW, gacc], writes=[gq])
        P.step("gpsimd", (lambda e, r=r, c=c: e.dma_start(
            out=dram[qname][r * 128:(r + 1) * 128, c * 1024:(c + 1) * 1024], in_=qA)),
            reads=[gq], dma="d_q")
    if DBG and wname == "d_w1T":
        gdd = Guard()
        P.step("gpsimd", (lambda e: e.dma_start(out=dram["dbg"][:, 0:48], in_=accc[:, 0:48])),
               reads=[gacc], writes=[gdd], dma="dbg7")
        P.step("gpsimd", (lambda e: e.dma_start(out=dram["dbg"][:, 48:64], in_=tcols[:, 0:16])),
               reads=[gacc], writes=[gdd], dma="dbg7")

def emit_phase_d(P, R, dram):
    R.new_section(P)
    _emit_gamma_quant(P, R, dram, "d_w1T", "d_wq1", 1024, 4096)
    _emit_gamma_quant(P, R, dram, "d_w2T", "d_wq2", 4096, 1024)
    q_tok = (P.dsem("d_q"), P.dsem("d_q").n)
    for s in range(NSEQ):
        R.new_section(P)
        ln = LNArea(P, R)
        xdbf = R.carve2(8 * T)
        hbf = R.carve2(32 * 256)
        w1b = R.carve2(8 * 128)
        w2b = R.carve2(16 * 128)
        b1c = R.carve4(32)
        b2c = R.carve4(8)
        gx, gh, gw1, gw2, gb1, gb2 = (Guard() for _ in range(6))
        P.step("sync", (lambda e: e.dma_start(out=b1c[:, 0:32], in_=dram["d_b1T"])),
               writes=[gb1], dma="d_b1")
        P.step("sync", (lambda e: e.dma_start(out=b2c[:, 0:8], in_=dram["d_b2T"])),
               writes=[gb2], dma="d_b2")
        for ic in range(4):
            for d in range(DT):
                P.step("vector", (lambda e, d=d, ic=ic, s=s: e.tensor_copy(
                    xdbf[:, d * T + ic * 512: d * T + ic * 512 + 512],
                    R.xT(s, d)[:, ic * 512:(ic + 1) * 512])),
                    reads=[R.gXT], writes=[gx])
        for ic8 in range(8):
            c0 = ic8 * 256
            for fb in range(32):
                P.step("gpsimd", (lambda e, fb=fb: e.dma_start(
                    out=w1b, in_=bass.AP(dram["d_wq1"].tensor, fb * 128,
                                         [[4096, 128], [128 * 4096, 8], [1, 128]]))),
                    extra_waits=[q_tok], writes=[gw1], dma="d_w1b")
                bk = fb % 2
                def hmm_(e, fb=fb, bk=bk, c0=c0):
                    last = None
                    for d in range(DT):
                        last = e.matmul(R.PB[bk][:, 0:256], w1b[:, d * 128:(d + 1) * 128],
                                        xdbf[:, d * T + c0: d * T + c0 + 256],
                                        start=(d == 0), stop=(d == DT - 1))
                    return last
                P.step("tensor", hmm_, reads=[gw1, gx], writes=[R.gPB[bk]])
                P.step("scalar", (lambda e, fb=fb, bk=bk: e.activation(
                    hbf[:, fb * 256:(fb + 1) * 256], R.PB[bk][:, 0:256], AF.Relu,
                    bias=b1c[:, fb:fb + 1])),
                    reads=[R.gPB[bk], gb1], writes=[gh])
            for dto in range(DT):
                bk = 2 + dto % 2
                for hh in range(2):
                    P.step("gpsimd", (lambda e, dto=dto, hh=hh: e.dma_start(
                        out=w2b, in_=bass.AP(dram["d_wq2"].tensor, (hh * 16 * 128) * 1024 + dto * 128,
                                             [[1024, 128], [128 * 1024, 16], [1, 128]]))),
                        extra_waits=[q_tok], writes=[gw2], dma="d_w2b")
                    def omm(e, bk=bk, hh=hh):
                        last = None
                        for ft in range(16):
                            last = e.matmul(R.PB[bk][:, 0:256], w2b[:, ft * 128:(ft + 1) * 128],
                                            hbf[:, (hh * 16 + ft) * 256:(hh * 16 + ft + 1) * 256],
                                            start=(hh == 0 and ft == 0), stop=(hh == 1 and ft == 15))
                        return last
                    P.step("tensor", omm, reads=[gw2, gh], writes=[R.gPB[bk]])
                P.step("vector", (lambda e, bk=bk, dto=dto, c0=c0, s=s: e.scalar_tensor_tensor(
                    R.xT(s, dto)[:, c0:c0 + 256], R.PB[bk][:, 0:256],
                    b2c[:, dto:dto + 1], R.xT(s, dto)[:, c0:c0 + 256],
                    OP.add, OP.add)),
                    reads=[R.gPB[bk], gb2], writes=[R.gXT])
        for ic in range(4):
            src_aps = [R.xT(s, d)[:, ic * 512:(ic + 1) * 512] for d in range(DT)]
            emit_ln_chunk(P, R, ln, src_aps, R.gXT, src_aps, R.gXT, pst_banks=(7, 4), bc_banks=(5, 6))


def declare_inputs(nc, consts, phases):
    dram = {}
    def din(name, shape):
        t = nc.dram_tensor(name, list(shape), F32, kind="ExternalInput")
        dram[name] = t.ap()
    for k, v in consts.items():
        din("c_" + k, v.shape)
    din("x_in", [NSEQ, D, T])
    dram["x_out"] = nc.dram_tensor("x_out", [NSEQ, D, T], F32, kind="ExternalOutput").ap()
    if DBG:
        dram["dbg"] = nc.dram_tensor("dbg", [128, 24576], F32, kind="ExternalOutput").ap()
    if "A" in phases:
        din("a_w_in", [1024, 4096])
        din("a_cwT", [128, 64])
        din("a_cbT", [128, 16])
        din("a_w_out", [2048, 1024])
    if any(p.startswith("B") for p in phases):
        din("b_wqkv", [6, 1024, 3072])
        din("b_qscol", [6, 128])
        din("b_kscol", [6, 128])
        din("b_wo", [6, 1024, 1024])
        din("b_lng", [6, 128, 8])
        din("b_lnb", [6, 128, 8])
        din("b_w1", [6, 1024, 5460])
        din("b_b1a", [6, 128, 22])
        din("b_b1g", [6, 128, 22])
        din("b_w2", [6, 2730, 1024])
        din("b_b2", [6, 128, 8])
        dram["b_xsnap"] = nc.dram_tensor("b_xsnap", [NSEQ * 128 * DT * T], F32, kind="Internal").ap()
    if "C" in phases:
        din("c_wqkT", [1024, 2048])
        din("c_wvT", [1024, 1024])
        din("c_woT", [1024, 1024])
        din("c_bqk", [16, 64, 2])
        din("c_bvrow", [1, 1024])
        din("c_boT", [128, 8])
        din("c_rel", [320, 16])
        din("c_gruw", [8, 64])
        din("c_grub", [8, 1])
        din("c_gruc", [16, 1])
        dram["c_g2"] = nc.dram_tensor("c_g2", [16, 4095], BF16, kind="Internal").ap()
        dram["c_avspill"] = nc.dram_tensor("c_avspill", [1024 * T], BF16, kind="Internal").ap()
    if "D" in phases:
        if DBG:
            dram["d_wq1"] = nc.dram_tensor("d_wq1", [1024, 4096], BF16, kind="ExternalOutput").ap()
        din("d_w1T", [1024, 4096])
        din("d_w2T", [4096, 1024])
        din("d_b1T", [128, 32])
        din("d_b2T", [128, 8])
        if not DBG:
            dram["d_wq1"] = nc.dram_tensor("d_wq1", [1024, 4096], BF16, kind="Internal").ap()
        dram["d_wq2"] = nc.dram_tensor("d_wq2", [4096, 1024], BF16, kind="Internal").ap()
    return dram

def build_core_program(phases):
    cx = contextlib.ExitStack()
    nc = bass.Bass("TRN2", target_bir_lowering=False)
    consts = host_constants()
    dram = declare_inputs(nc, consts, phases)
    with cx:
        P = Prog(nc, cx)
        R = Resident(P, dram)
        R.load_x(P, dram["x_in"])
        for ph in phases:
            P.barrier()
            if ph == "A":
                emit_phase_a(P, R, dram)
            elif ph == "Ba":
                emit_b_attn(P, R, dram, 0, 0)
                emit_b_attn(P, R, dram, 1, 0)
            elif ph == "C":
                emit_phase_c(P, R, dram)
            elif ph == "D":
                emit_phase_d(P, R, dram)
            elif ph.startswith("B"):
                nl = 6 if ph == "B" else int(ph[1:])
                emit_phase_b(P, R, dram, nlayers=nl, final=(ph == "B"))
        P.barrier()
        R.store_x(P, dram["x_out"])
        P.finalize()
    return nc, consts

def shard_inputs(inputs, consts, phases, core):
    x = inputs["L0_attn_x"]
    xs = x[core * NSEQ:(core + 1) * NSEQ]
    m = {"x_in": np.ascontiguousarray(xs.transpose(0, 2, 1)).astype(np.float32)}
    for k, v in consts.items():
        m["c_" + k] = v
    if "A" in phases:
        m["a_w_in"] = inputs["L0_attn_W_in"]
        cw = inputs["L0_attn_conv_w"].reshape(2048, 4).reshape(16, 128, 4)
        m["a_cwT"] = np.ascontiguousarray(cw.transpose(1, 0, 2)).reshape(128, 64)
        m["a_cbT"] = np.ascontiguousarray(inputs["L0_attn_conv_b"].reshape(16, 128).T)
        m["a_w_out"] = inputs["L0_attn_W_out"]
    if any(p.startswith("B") for p in phases):
        m["b_wqkv"] = inputs["L0_ffn_Wqkv"]
        m["b_qscol"] = np.ascontiguousarray(np.concatenate([inputs["L0_ffn_q_scale"]] * 2, axis=1))
        m["b_kscol"] = np.ascontiguousarray(np.concatenate([inputs["L0_ffn_k_scale"]] * 2, axis=1))
        m["b_wo"] = inputs["L0_ffn_Wo"]
        m["b_lng"] = np.ascontiguousarray(inputs["L0_ffn_ln_g"].reshape(6, 8, 128).transpose(0, 2, 1))
        m["b_lnb"] = np.ascontiguousarray(inputs["L0_ffn_ln_b"].reshape(6, 8, 128).transpose(0, 2, 1))
        m["b_w1"] = inputs["L0_ffn_W1"]
        b1 = inputs["L0_ffn_b1"]
        b1a = np.zeros((6, 2816), dtype=np.float32); b1a[:, :2730] = b1[:, :2730]
        b1g = np.zeros((6, 2816), dtype=np.float32); b1g[:, :2730] = b1[:, 2730:]
        m["b_b1a"] = np.ascontiguousarray(b1a.reshape(6, 22, 128).transpose(0, 2, 1))
        m["b_b1g"] = np.ascontiguousarray(b1g.reshape(6, 22, 128).transpose(0, 2, 1))
        m["b_w2"] = inputs["L0_ffn_W2"]
        m["b_b2"] = np.ascontiguousarray(inputs["L0_ffn_b2"].reshape(6, 8, 128).transpose(0, 2, 1))
    if "C" in phases:
        WqT = np.ascontiguousarray(inputs["L1_attn_Wq"].T)
        WkT = np.ascontiguousarray(inputs["L1_attn_Wk"].T)
        wqk = np.zeros((1024, 2048), dtype=np.float32)
        for h in range(16):
            wqk[:, 128 * h: 128 * h + 64] = WqT[:, 64 * h: 64 * h + 64]
            wqk[:, 128 * h + 64: 128 * h + 128] = WkT[:, 64 * h: 64 * h + 64]
        m["c_wqkT"] = wqk
        m["c_wvT"] = np.ascontiguousarray(inputs["L1_attn_Wv"].T)
        m["c_woT"] = np.ascontiguousarray(inputs["L1_attn_Wo"].T)
        bq, bk = inputs["L1_attn_bq"], inputs["L1_attn_bk"]
        m["c_bqk"] = np.ascontiguousarray(
            np.stack([bq.reshape(16, 64), bk.reshape(16, 64)], axis=2))
        m["c_bvrow"] = inputs["L1_attn_bv"].reshape(1, 1024)
        m["c_boT"] = np.ascontiguousarray(inputs["L1_attn_bo"].reshape(8, 128).T)
        m["c_rel"] = inputs["L1_attn_rel_embed"]
        m["c_gruw"] = inputs["L1_attn_gru_W"]
        m["c_grub"] = inputs["L1_attn_gru_b"].reshape(8, 1)
        m["c_gruc"] = inputs["L1_attn_gru_const"].reshape(16, 1)
    if "D" in phases:
        m["d_w1T"] = np.ascontiguousarray(inputs["L1_ffn_W1"].T)
        m["d_w2T"] = np.ascontiguousarray(inputs["L1_ffn_W2"].T)
        m["d_b1T"] = np.ascontiguousarray(inputs["L1_ffn_b1"].reshape(32, 128).T)
        m["d_b2T"] = np.ascontiguousarray(inputs["L1_ffn_b2"].reshape(8, 128).T)
    return m

def run_phases(inputs, phases):
    nc, consts = build_core_program(phases)
    in_maps = [shard_inputs(inputs, consts, phases, core) for core in range(NCORES)]
    res = run_bass_kernel_spmd(nc, in_maps, core_ids=list(range(NCORES)))
    out = np.empty((16, T, D), dtype=np.float32)
    for core in range(NCORES):
        xo = res.results[core]["x_out"]
        out[core * NSEQ:(core + 1) * NSEQ] = xo.transpose(0, 2, 1)
    return out

ALL_PHASES = ["A", "B", "C", "D"]

def kernel(**inputs):
    return run_phases(inputs, ALL_PHASES)
